# version 11 (final)
# baseline (speedup 1.0000x reference)
import contextlib
import math
import resource
import numpy as np

try:
    resource.setrlimit(resource.RLIMIT_STACK,
                       (resource.RLIM_INFINITY, resource.RLIM_INFINITY))
except (ValueError, OSError):
    try:
        _hard = resource.getrlimit(resource.RLIMIT_STACK)[1]
        resource.setrlimit(resource.RLIMIT_STACK, (_hard, _hard))
    except (ValueError, OSError):
        pass

import concourse.bass as bass
import concourse.mybir as mybir
from concourse.bass_utils import run_bass_kernel_spmd

F32 = mybir.dt.float32
BF16 = mybir.dt.bfloat16
I8 = mybir.dt.int8
AF = mybir.ActivationFunctionType
ALU = mybir.AluOpType

MAGIC = 1.5 * 2.0 ** 23
INV2PI = 1.0 / (2.0 * math.pi)
TWOPI = 2.0 * math.pi
HALFPI = 0.5 * math.pi
PICLAMP = 3.14159


class CFG:
    def __init__(self, small=False, n_cores=8):
        self.n_cores = n_cores
        if not small:
            self.D, self.SEQ, self.NOWN, self.DFF = 2048, 2048, 1024, 8192
            self.H0, self.NH, self.ST = 16, 32, 256
        else:
            self.D, self.SEQ, self.NOWN, self.DFF = 512, 1024, 512, 4096
            self.H0, self.NH, self.ST = 4, 8, 128
        self.HD0, self.L1HD, self.CH, self.KCONV, self.DEG = 128, 64, 64, 4, 4
        self.CONVD = self.D + 2 * self.ST
        self.PROJ = self.D + self.CONVD + self.NH
        self.TOWN = self.NOWN // 128
        self.TKV = self.SEQ // 128
        self.FCH = self.D // 128
        self.NOC = self.DFF // 128
        self.NCH = self.NOWN // self.CH
        self.NCT = self.CONVD // 128
        self.NG = max(1, self.NH * self.L1HD // 512)
        self.HPG = self.NH // self.NG
        self.QW = min(512, self.NOWN)
        self.NQC = self.NOWN // self.QW
        self.NST = self.ST // 128
        self.G = 128 // self.NOC if self.NOC < 128 else 1
        self.groups = [[2 * i, 2 * i + 1] for i in range(n_cores // 2)]


def _span(ap):
    try:
        name = ap.tensor.name
    except Exception:
        return None
    off = ap.offset
    ext = 1
    for s, cnt in ap.ap:
        if cnt > 1:
            ext += abs(s) * (cnt - 1)
    return (name, off, off + ext)


def _olap(a, b):
    return a[0] == b[0] and a[1] < b[2] and b[1] < a[2]


class _Dummy:
    def then_inc(self, *a, **k):
        return self


_DUM = _Dummy()


class Sem:
    def __init__(self, h):
        self.h = h
        self.n = 0


class Eng:
    def __init__(self, bld, name):
        self.b = bld
        self.name = name
        self.pw = []
        self.pr = []
        self.last = None

    def _sec(self, fn):
        if self.b.dead:
            return _DUM
        out = []
        getattr(self.b.blk, self.name)(lambda e: out.append(fn(e)))
        return out[0]

    def drain(self):
        self._sec(lambda e: e.drain())
        self.pw, self.pr = [], []

    def _do(self, fn, writes, reads):
        if self.b.dead:
            self.last = _DUM
            return _DUM
        ws = [s for s in (_span(a) for a in writes) if s]
        rs = [s for s in (_span(a) for a in reads) if s]
        need = False
        for w in ws:
            if any(_olap(w, p) for p in self.pw) or any(_olap(w, p) for p in self.pr):
                need = True
                break
        if not need:
            for r in rs:
                if any(_olap(r, p) for p in self.pw):
                    need = True
                    break
        if need:
            self.drain()
        self.pw += ws
        self.pr += rs
        self.last = self._sec(fn)
        return self.last

    def tt(self, out, a, bb, op):
        return self._do(lambda e: e.tensor_tensor(out, a, bb, op), [out], [a, bb])

    def ts(self, out, in0, s1, s2=None, op0=ALU.mult, op1=ALU.bypass, accum=None):
        kw = {'accum_out': accum} if accum is not None else {}
        w = [out] + ([accum] if accum is not None else [])
        r = [in0] + [s for s in (s1, s2) if isinstance(s, bass.AP)]
        return self._do(lambda e: e.tensor_scalar(out, in0, s1, s2, op0, op1, **kw), w, r)

    def stt(self, out, in0, sc, in1, op0, op1):
        r = [in0, in1] + ([sc] if isinstance(sc, bass.AP) else [])
        return self._do(lambda e: e.scalar_tensor_tensor(out, in0, sc, in1, op0, op1),
                        [out], r)

    def act(self, out, in_, func, bias=0.0, scale=1.0, accum=None):
        kw = {'accum_out': accum} if accum is not None else {}
        w = [out] + ([accum] if accum is not None else [])
        r = [in_] + [x for x in (bias, scale) if isinstance(x, bass.AP)]
        return self._do(lambda e: e.activation(out, in_, func, bias=bias, scale=scale, **kw), w, r)

    def copy(self, out, in_):
        if self.name == 'scalar':
            return self._do(lambda e: e.copy(out, in_), [out], [in_])
        return self._do(lambda e: e.tensor_copy(out, in_), [out], [in_])

    def memset(self, ap, cval):
        return self._do(lambda e: e.memset(ap, cval), [ap], [])

    def red(self, out, in_, op, abs_=False):
        return self._do(lambda e: e.tensor_reduce(out, in_, mybir.AxisListType.X, op,
                                                  apply_absolute_value=abs_), [out], [in_])

    def recip(self, out, in_):
        return self._do(lambda e: e.reciprocal(out, in_), [out], [in_])

    def scan(self, out, d0, d1, init, op0, op1):
        return self._do(lambda e: e.tensor_tensor_scan(out, d0, d1, init, op0, op1),
                        [out], [d0, d1])

    def mm(self, out, lhsT, rhs, start=True, stop=True):
        return self._do(lambda e: e.matmul(out, lhsT, rhs, start=start, stop=stop),
                        [out], [lhsT, rhs])

    def tr(self, out, in_, ident):
        return self._do(lambda e: e.transpose(out, in_, ident), [out], [in_, ident])

    def dma(self, out, in_, slow=False):
        if slow:
            return self._do(lambda e: e.dma_start(out=out, in_=in_,
                                                  allow_slow_non_contiguous=True),
                            [out], [in_])
        return self._do(lambda e: e.dma_start(out=out, in_=in_), [out], [in_])

    def affsel(self, out, in_, pattern, cmp, fill=0.0, base=0, cm=1):
        return self._do(lambda e: e.affine_select(out, in_, pattern=pattern,
                                                  compare_op=cmp, fill=fill, base=base,
                                                  channel_multiplier=cm), [out], [in_])

    def iota(self, out, pattern, base=0, cm=1):
        return self._do(lambda e: e.iota(out, pattern=pattern, base=base,
                                         channel_multiplier=cm,
                                         allow_small_or_imprecise_dtypes=True), [out], [])

    def cc(self, kind, op, groups, ins, outs):
        return self._do(lambda e: e.collective_compute(kind, op, replica_groups=groups,
                                                       ins=ins, outs=outs), [], [])

    def wait(self, sem, val=None):
        if self.b.dead:
            return
        v = sem.n if val is None else val
        self._sec(lambda e: e.wait_ge(sem.h, v))
        if self.name == 'sync' and 'dD' in self.b.sems:
            self.b.hist['_dbound'] = self.b.sems['dD'].n


class B:
    def __init__(self, nc, cfg, stack, blk, cuttag=None):
        self.nc = nc
        self.c = cfg
        self.st = stack
        self.blk = blk
        self.dead = False
        self.cuttag = cuttag
        self.T = Eng(self, 'tensor')
        self.V = Eng(self, 'vector')
        self.S = Eng(self, 'scalar')
        self.G = Eng(self, 'gpsimd')
        self.Y = Eng(self, 'sync')
        self.sems = {}
        self.hist = {}

    def cut(self, tag):
        if self.cuttag is not None and self.cuttag == tag:
            self.dead = True

    def sem(self, name):
        if name not in self.sems:
            self.sems[name] = Sem(self.st.enter_context(self.nc.semaphore(name)))
        return self.sems[name]

    def esem(self, eng):
        return self.sem('e_' + eng.name)

    def done(self, eng, k=1):
        s = self.esem(eng)
        s.n += k
        eng.last.then_inc(s.h, k)
        return s

    def dinc(self, eng, name='dD'):
        s = self.sem(name)
        s.n += 16
        eng.last.then_inc(s.h, 16)
        return s

    def guard(self, key, wait_eng):
        for sem, val in self.hist.get(key, ()):
            wait_eng.wait(sem, val)

    def mark(self, key, eng):
        s = self.esem(eng)
        self.hist[key] = [(s, s.n)]

    def markadd(self, key, eng):
        s = self.esem(eng)
        self.hist.setdefault(key, []).append((s, s.n))

    def dwait(self, eng):
        dD = self.sem('dD')
        if self.hist.get('_dbound', 0) != dD.n and eng.name != 'sync':
            self.Y.wait(dD)
        eng.wait(dD)

    def pguard(self, bank):
        self.guard(('pb', bank), self.T)

    def pmark(self, bank, eng):
        self.mark(('pb', bank), eng)

    def pmarkadd(self, bank, eng):
        self.markadd(('pb', bank), eng)


def repcol(col_ap, n):
    t = col_ap.tensor
    prow = 1
    for d in t.shape[1:]:
        prow *= d
    return bass.AP(t, col_ap.offset, [[prow, col_ap.ap[0][1]], [0, n]])


def build_program(cfg, cut=None):
    c = cfg
    nc = bass.Bass('TRN2', target_bir_lowering=False,
                   dynamic_dma_scratch_size=1024)
    D, SEQ, NOWN, DFF = c.D, c.SEQ, c.NOWN, c.DFF
    dram = {}

    def din(name, shape, dt=F32):
        dram[name] = nc.dram_tensor(name, shape, dt, kind='ExternalInput').ap()
        return dram[name]

    din('x_seq', [SEQ, D])
    din('pos_off', [128, 1])
    din('pos_off2', [128, 1])
    din('flag_h1', [128, 1])
    din('m_self', [128, 128], BF16)
    din('H128', [128, 128], BF16)
    din('HJ', [c.NOC, c.NOC], BF16)
    din('w_qkv', [D, 3 * D])
    din('w_out', [D, D])
    din('ln_scale', [1, c.HD0])
    din('ln_bias', [1, c.HD0])
    din('wgT', [D, DFF])
    din('wuT', [D, DFF])
    din('wdTp', [DFF, D])
    din('norm_w', [1, D])
    din('w_in', [D, c.PROJ])
    din('conv_w', [1, c.CONVD * c.KCONV])
    din('conv_b', [1, c.CONVD])
    din('dt_bias', [1, c.NH])
    din('A_log', [1, c.NH])
    din('D_skip', [1, c.NH])
    din('gnorm_w', [1, D])
    din('w_out1', [D, D])
    din('wo_w', [D, D])
    din('WgT1', [D, DFF])
    din('WuT1', [D, DFF])
    din('WdT1', [DFF, D])
    y_out = nc.dram_tensor('y', [NOWN, D], F32, kind='ExternalOutput').ap()

    zspill = nc.dram_tensor('zspill', [NOWN, D], BF16)
    cc1_in = nc.dram_tensor('cc1_in', [4, c.CONVD], F32)
    cc1_out = nc.dram_tensor('cc1_out', [8, c.CONVD], F32)
    SW = c.NH * c.L1HD
    cc2_in = nc.dram_tensor('cc2_in', [c.NST * 128, SW], F32)
    cc2_out = nc.dram_tensor('cc2_out', [2 * c.NST * 128, SW], F32)

    with contextlib.ExitStack() as st0:
        blk = st0.enter_context(nc.Block())
        b = B(nc, c, st0, blk, cuttag=cut)
        T, V, S, G, Y = b.T, b.V, b.S, b.G, b.Y
        dD = b.sem('dD')
        dO = b.sem('dO')
        ccsem = b.sem('ccS')

        def sb(name, shape, dt, stk=None, side=None):
            if side:
                return (stk or st0).enter_context(
                    nc.sbuf_tensor(name, shape, dt, side=side))
            return (stk or st0).enter_context(nc.sbuf_tensor(name, shape, dt))

        res = sb('res', [128, c.TOWN, D], F32)
        onesf = sb('onesf', [128, 160], F32)
        onesb = sb('onesb', [128, 8], BF16)
        identf = sb('identf', [128, 128], F32)
        identb = sb('identb', [128, 128], BF16)
        cst = sb('cst', [128, 8], F32)
        flagc = sb('flagc', [128, 1], F32)
        posoff = sb('posoff', [128, 1], F32)
        posoff2 = sb('posoff2', [128, 1], F32)
        mself = sb('mself', [128, 128], BF16)
        h128 = sb('h128', [128, 128], BF16)
        hj = sb('hj', [c.NOC, c.NOC], BF16)
        junkV = sb('junkV', [128, max(D, 2048)], BF16)
        junkS = sb('junkS', [128, max(D, 2048)], BF16)
        junk2f = sb('junk2f', [128, max(D, 2048)], F32)
        colA = sb('colA', [128, 16], F32)
        colB = sb('colB', [128, 16], F32)
        swcols = sb('swcols', [128, 8], F32)
        swhalf = sb('swhalf', [1, 4], F32)
        invs = sb('invs', [128, c.TOWN], F32)
        dusc = sb('dusc', [128, c.TOWN], F32)
        dsc = sb('dsc', [128, c.TOWN], F32)
        wf32 = sb('wf32', [128, 2, 512], F32)
        rowsc = sb('rowsc', [1, 520], F32)
        lnsc = sb('lnsc', [128, 128], F32)
        lnbc = sb('lnbc', [128, 128], F32)
        bsums = sb('bsums', [128, 4 * c.TOWN], F32)
        bsqs = sb('bsqs', [128, 4 * c.TOWN], F32)

        P = [st0.enter_context(nc.psum_tensor(f'P{i}', [128, 512], F32))
             for i in range(8)]

        def pbf128(bank):
            return P[bank][:, 0:64].bitcast(BF16)

        def pbf64(bank):
            return P[bank][0:128, 0:64].bitcast(BF16)[0:64, 0:128]

        def stream_f32(dst_fn, src2d, r0, R, c0, C_):
            for i in range(R // 128):
                par = b.hist.get('_wq', 0) % 2
                b.hist['_wq'] = b.hist.get('_wq', 0) + 1
                b.guard(('wscr', par), Y)
                Y.dma(wf32[:, par, 0:C_],
                      src2d[r0 + i * 128:r0 + (i + 1) * 128, c0:c0 + C_])
                b.dinc(Y)
                eng = dst_fn(i, wf32[:, par, 0:C_])
                b.mark(('wscr', par), eng)

        def load_cast(dst3, src2d, r0, R, c0, C_, quant=None, swi=0):
            def f(i, scr):
                d = dst3[:, i, 0:C_]
                if quant is None:
                    b.dwait(V)
                    V.copy(d, scr)
                    b.done(V)
                    return V
                if quant == 'sign':
                    b.dwait(S)
                    S.act(d, scr, AF.Sign)
                    b.done(S)
                    return S
                b.dwait(S)
                S.wait(b.esem(V))
                S.act(junkS[:, 0:C_], scr, AF.Sign, bias=swcols[:, swi:swi + 1])
                S.act(d, scr, AF.Sign, bias=swcols[:, swi + 1:swi + 2])
                b.done(S)
                V.wait(b.esem(S))
                V.tt(d, d, junkS[:, 0:C_], ALU.add)
                b.done(V)
                return V
            stream_f32(f, src2d, r0, R, c0, C_)

        def tr128(dst, src, bank, bf=False):
            b.pguard(bank)
            if bf:
                T.tr(pbf128(bank), src, identb[:, :])
            else:
                T.tr(P[bank][:, 0:128], src, identf[:, :])
            b.done(T)
            V.wait(b.esem(T))
            V.copy(dst, pbf128(bank) if bf else P[bank][:, 0:128])
            b.done(V)
            b.pmark(bank, V)

        def ln_stats(tile_f32, negm_col, rstd_col, eps_col, width=D):
            V.ts(junkV[:, 0:width], tile_f32, 0.0, None, ALU.add, ALU.add,
                 accum=colA[:, 0:1])
            b.done(V)
            S.act(junkS[:, 0:width], tile_f32, AF.Square, accum=colA[:, 1:2])
            b.done(S)
            V.wait(b.esem(S))
            V.ts(negm_col, colA[:, 0:1], -1.0 / width)
            V.ts(colA[:, 2:3], colA[:, 1:2], 1.0 / width)
            V.ts(colA[:, 3:4], negm_col, negm_col, op0=ALU.mult)
            V.stt(colA[:, 4:5], colA[:, 3:4], -1.0, colA[:, 2:3], ALU.mult, ALU.add)
            b.done(V)
            S.wait(b.esem(V))
            S.act(colA[:, 5:6], colA[:, 4:5], AF.Sqrt, bias=eps_col)
            b.done(S)
            V.wait(b.esem(S))
            V.recip(rstd_col, colA[:, 5:6])
            b.done(V)

        def branch_ln_add(brn, bidx, tile, nsum=4, width=D):
            V.red(colA[:, 0:1], bsums[:, tile * 4:tile * 4 + nsum], ALU.add)
            V.red(colA[:, 1:2], bsqs[:, tile * 4:tile * 4 + nsum], ALU.add)
            V.ts(colB[:, 0:1], colA[:, 0:1], -1.0 / width)
            V.ts(colA[:, 2:3], colA[:, 1:2], 1.0 / width)
            V.ts(colA[:, 3:4], colB[:, 0:1], colB[:, 0:1], op0=ALU.mult)
            V.stt(colA[:, 4:5], colA[:, 3:4], -1.0, colA[:, 2:3], ALU.mult, ALU.add)
            b.done(V)
            S.wait(b.esem(V))
            S.act(colA[:, 5:6], colA[:, 4:5], AF.Sqrt, bias=cst[:, 0:1])
            b.done(S)
            V.wait(b.esem(S))
            V.recip(colB[:, 1:2], colA[:, 5:6])
            V.stt(junk2f[:, 0:width], brn[:, bidx, 0:width], colB[:, 0:1],
                  repcol(colB[:, 1:2], width), ALU.add, ALU.mult)
            V.tt(res[:, tile, 0:width], res[:, tile, 0:width],
                 junk2f[:, 0:width], ALU.add)
            b.done(V)

        def row_to_allparts(dst, src_dram_slice, n):
            Y.wait(b.esem(V))
            b.guard(('rowsc',), Y)
            Y.dma(rowsc[0:1, 0:n], src_dram_slice)
            b.dinc(Y)
            b.pguard(7)
            b.dwait(T)
            T.mm(P[7][:, 0:n], onesf[0:1, 0:128], rowsc[0:1, 0:n])
            b.done(T)
            b.mark(('rowsc',), T)
            V.wait(b.esem(T))
            V.copy(dst, P[7][:, 0:n])
            b.done(V)
            b.pmark(7, V)

        Y.dma(res[:, :, :],
              dram['x_seq'][0:NOWN, :].rearrange('(t p) d -> p t d', p=128))
        b.dinc(Y)
        for nm, dst in (('pos_off', posoff), ('pos_off2', posoff2),
                        ('flag_h1', flagc), ('m_self', mself),
                        ('H128', h128), ('HJ', hj)):
            Y.dma(dst[:, :], dram[nm])
            b.dinc(Y)
        V.memset(onesf[:, :], 1.0)
        V.memset(onesb[:, :], 1.0)
        V.memset(cst[:, 0:1], 1e-5)
        V.memset(cst[:, 1:2], 1e-8)
        V.memset(cst[:, 2:3], 1.0)
        V.memset(cst[:, 3:4], 0.0)
        b.done(V)
        G.wait(b.esem(V))
        G.affsel(identf[:, :], onesf[:, 0:128], [[-1, 128]], ALU.is_equal)
        b.done(G)
        V.wait(b.esem(G))
        V.copy(identb[:, :], identf[:, :])
        b.done(V)
        row_to_allparts(lnsc[:, 0:c.HD0], dram['ln_scale'], c.HD0)
        row_to_allparts(lnbc[:, 0:c.HD0], dram['ln_bias'], c.HD0)

        b.cut('p0')
        accs = sb('accs', [128, 4], F32)
        for wi, (nm, R, C_) in enumerate((('wgT', D, DFF), ('wuT', D, DFF),
                                          ('wdTp', DFF, D))):
            V.memset(accs[:, wi:wi + 1], 0.0)
            b.done(V)
            PC = min(512, C_)

            def fA(i, scr, wi=wi, PC=PC):
                b.dwait(S)
                S.wait(b.esem(V))
                S.act(junkS[:, 0:PC], scr, AF.Abs, accum=colB[:, 8:9])
                b.done(S)
                V.wait(b.esem(S))
                V.tt(accs[:, wi:wi + 1], accs[:, wi:wi + 1], colB[:, 8:9], ALU.add)
                b.done(V)
                return V
            for cc_ in range(C_ // PC):
                stream_f32(fA, dram[nm], 0, R, cc_ * PC, PC)
            b.pguard(7)
            T.wait(b.esem(V))
            T.mm(P[7][0:1, 0:1], accs[:, wi:wi + 1], onesf[:, 0:1])
            b.done(T)
            V.wait(b.esem(T))
            V.ts(rowsc[0:1, 4:5], P[7][0:1, 0:1], 1.0 / (R * C_))
            V.ts(rowsc[0:1, 4:5], rowsc[0:1, 4:5], 1e-5, None, ALU.max)
            V.ts(swhalf[0:1, wi:wi + 1], rowsc[0:1, 4:5], 0.5)
            b.done(V)
            b.pmark(7, V)
            b.pguard(7)
            T.wait(b.esem(V))
            T.mm(P[7][:, 0:1], onesf[0:1, 0:128], swhalf[0:1, wi:wi + 1])
            b.done(T)
            V.wait(b.esem(T))
            V.ts(swcols[:, 2 * wi:2 * wi + 1], P[7][:, 0:1], -1.0)
            V.copy(swcols[:, 2 * wi + 1:2 * wi + 2], P[7][:, 0:1])
            b.done(V)
            b.pmark(7, V)

        b.cut('pA')
        stB = contextlib.ExitStack()
        with stB:
            attn_oT = sb('attn_oT', [128, c.H0, NOWN], BF16, stB)
            pcol = sb('pcol', [128, 1], F32, stB)
            stB1 = contextlib.ExitStack()

            def _SBB1(nm, shape, dt, _ignored=None):
                return sb(nm, shape, dt, stB1)
            xnT = sb('xnT', [128, c.FCH, SEQ], BF16, stB1)
            ctt = _SBB1('ctt', [128, 64], F32, stB)
            stt_ = _SBB1('stt_', [128, 64], F32, stB)
            sc64 = _SBB1('sc64', [128, 64], F32, stB)
            invf = _SBB1('invf', [128, 64], F32, stB)
            xtile = _SBB1('xtile', [128, 1, D], F32, stB)
            kT = _SBB1('kT', [128, 2, SEQ], BF16, stB)
            qT = _SBB1('qT', [128, 2, NOWN], BF16, stB)
            vbuf = _SBB1('vbuf', [128, c.TKV, 2, 128], BF16, stB)
            wslu = _SBB1('wslu', [128, c.FCH, 256], BF16, stB)
            wslq = wslk = wslv = wslu
            qk_tm = junk2f[:, 256:512]
            G.iota(invf[0:1, 0:64], [[1, 64]], cm=0)
            b.done(G)
            G.iota(pcol[:, 0:1], [[1, 1]], base=0, cm=1)
            b.done(G)
            V.wait(b.esem(G))
            S.wait(b.esem(G))
            S.act(invf[0:1, 0:64], invf[0:1, 0:64], AF.Exp,
                  scale=-2.0 * math.log(10000.0) / c.HD0)
            b.done(S)
            b.pguard(7)
            T.wait(b.esem(S))
            T.mm(P[7][:, 0:64], onesf[0:1, 0:128], invf[0:1, 0:64])
            b.done(T)
            V.wait(b.esem(T))
            V.copy(invf[:, 0:64], P[7][:, 0:64])
            b.done(V)
            b.pmark(7, V)

            def rope_tables(slot):
                base = float((slot if slot < c.TOWN else slot - c.TOWN) * 128)
                po = posoff if slot < c.TOWN else posoff2
                V.ts(colB[:, 2:3], pcol[:, 0:1], base, None, ALU.add)
                V.tt(colB[:, 2:3], colB[:, 2:3], po[:, 0:1], ALU.add)
                V.tt(ctt[:, :], repcol(colB[:, 2:3], 64), invf[:, 0:64], ALU.mult)
                V.ts(stt_[:, :], ctt[:, :], INV2PI, MAGIC, ALU.mult, ALU.add)
                V.ts(stt_[:, :], stt_[:, :], MAGIC, None, ALU.subtract)
                V.stt(stt_[:, :], stt_[:, :], -TWOPI, ctt[:, :], ALU.mult, ALU.add)
                V.ts(stt_[:, :], stt_[:, :], PICLAMP, -PICLAMP, ALU.min, ALU.max)
                V.ts(ctt[:, :], ctt[:, :], 1.0, HALFPI, ALU.mult, ALU.add)
                V.ts(sc64[:, :], ctt[:, :], INV2PI, MAGIC, ALU.mult, ALU.add)
                V.ts(sc64[:, :], sc64[:, :], MAGIC, None, ALU.subtract)
                V.stt(ctt[:, :], sc64[:, :], -TWOPI, ctt[:, :], ALU.mult, ALU.add)
                V.ts(ctt[:, :], ctt[:, :], PICLAMP, -PICLAMP, ALU.min, ALU.max)
                b.done(V)
                S.wait(b.esem(V))
                S.act(stt_[:, :], stt_[:, :], AF.Sin)
                S.act(ctt[:, :], ctt[:, :], AF.Sin)
                b.done(S)

            for kt in range(c.TKV):
                par = 0
                b.guard(('xtile', par), Y)
                Y.dma(xtile[:, par, :], dram['x_seq'][kt * 128:(kt + 1) * 128, :])
                b.dinc(Y)
                b.dwait(V)
                b.dwait(S)
                ln_stats(xtile[:, par, :], colB[:, 0:1], colB[:, 1:2], cst[:, 0:1])
                V.stt(junk2f[:, 0:D], xtile[:, par, :], colB[:, 0:1],
                      repcol(colB[:, 1:2], D), ALU.add, ALU.mult)
                b.done(V)
                b.mark(('xtile', par), V)
                T.wait(b.esem(V))
                for f in range(c.FCH):
                    tr128(xnT[:, f, kt * 128:(kt + 1) * 128],
                          junk2f[:, f * 128:(f + 1) * 128], 6 + (f % 2))

            b.cut('pB1')
            for sp in range(c.H0 // 2):
                def loadqkv(cbase):
                    tEs = b.esem(T)
                    V.wait(tEs, tEs.n)
                    load_cast(wslu, dram['w_qkv'], 0, D, cbase, 256)

                def proj_tile(slot, wsl_, pb):
                    b.pguard(pb)
                    T.wait(b.esem(V))
                    for f in range(c.FCH):
                        T.mm(P[pb][:, 0:256], xnT[:, f, slot * 128:(slot + 1) * 128],
                             wsl_[:, f, :], start=(f == 0), stop=(f == c.FCH - 1))
                    b.done(T)

                def rope_ln_store(slot, dstT):
                    ropq = junk2f
                    rope_tables(slot)
                    V.wait(b.esem(S))
                    for h2 in range(2):
                        x1 = qk_tm[:, h2 * 128:h2 * 128 + 64]
                        x2 = qk_tm[:, h2 * 128 + 64:h2 * 128 + 128]
                        o1 = ropq[:, h2 * 128:h2 * 128 + 64]
                        o2 = ropq[:, h2 * 128 + 64:h2 * 128 + 128]
                        V.tt(o1, x1, ctt[:, :], ALU.mult)
                        V.stt(sc64[:, :], x2, -1.0, stt_[:, :], ALU.mult, ALU.mult)
                        V.tt(o1, o1, sc64[:, :], ALU.add)
                        V.tt(o2, x1, stt_[:, :], ALU.mult)
                        V.tt(sc64[:, :], x2, ctt[:, :], ALU.mult)
                        V.tt(o2, o2, sc64[:, :], ALU.add)
                        b.done(V)
                    for h2 in range(2):
                        rh = ropq[:, h2 * 128:(h2 + 1) * 128]
                        V.ts(junkV[:, 0:128], rh, 0.0, None, ALU.add, ALU.add,
                             accum=colA[:, 6:7])
                        b.done(V)
                        S.wait(b.esem(V))
                        S.act(junkS[:, 0:128], rh, AF.Square, accum=colA[:, 7:8])
                        b.done(S)
                        V.wait(b.esem(S))
                        V.ts(colA[:, 8:9], colA[:, 6:7], -1.0 / 128)
                        V.ts(colA[:, 9:10], colA[:, 7:8], 1.0 / 128)
                        V.ts(colA[:, 10:11], colA[:, 8:9], colA[:, 8:9], op0=ALU.mult)
                        V.stt(colA[:, 11:12], colA[:, 10:11], -1.0, colA[:, 9:10],
                              ALU.mult, ALU.add)
                        b.done(V)
                        S.wait(b.esem(V))
                        S.act(colA[:, 12:13], colA[:, 11:12], AF.Sqrt, bias=cst[:, 0:1])
                        b.done(S)
                        V.wait(b.esem(S))
                        V.recip(colA[:, 13:14], colA[:, 12:13])
                        V.stt(qk_tm[:, h2 * 128:(h2 + 1) * 128], rh, colA[:, 8:9],
                              repcol(colA[:, 13:14], 128), ALU.add, ALU.mult)
                        V.tt(qk_tm[:, h2 * 128:(h2 + 1) * 128],
                             qk_tm[:, h2 * 128:(h2 + 1) * 128], lnsc[:, 0:128], ALU.mult)
                        V.tt(qk_tm[:, h2 * 128:(h2 + 1) * 128],
                             qk_tm[:, h2 * 128:(h2 + 1) * 128], lnbc[:, 0:128], ALU.add)
                        b.done(V)
                        T.wait(b.esem(V))
                        tr128(dstT[:, h2, slot * 128:(slot + 1) * 128],
                              qk_tm[:, h2 * 128:(h2 + 1) * 128], 6 + h2)

                loadqkv(D + sp * 256)
                for slot in range(c.TKV):
                    pb = slot % 2
                    proj_tile(slot, wslk, pb)
                    S.wait(b.esem(T))
                    S.copy(qk_tm[:, :], P[pb][:, 0:256])
                    b.done(S)
                    b.pmark(pb, S)
                    V.wait(b.esem(S))
                    rope_ln_store(slot, kT)
                loadqkv(2 * D + sp * 256)
                for slot in range(c.TKV):
                    pb = slot % 2
                    proj_tile(slot, wslv, pb)
                    V.wait(b.esem(T))
                    V.copy(vbuf[:, slot, :, :].rearrange('p a w -> p (a w)'),
                           P[pb][:, 0:256])
                    b.done(V)
                    b.pmark(pb, V)
                loadqkv(sp * 256)
                for slot in range(c.TOWN):
                    pb = slot % 2
                    proj_tile(slot, wslq, pb)
                    S.wait(b.esem(T))
                    S.copy(qk_tm[:, :], P[pb][:, 0:256])
                    b.done(S)
                    b.pmark(pb, S)
                    V.wait(b.esem(S))
                    rope_ln_store(slot, qT)

                for h2 in range(2):
                    for qc in range(c.NQC):
                        QW = c.QW
                        qblocks = QW // 128
                        kslots = [(t, 'own') for t in range(qc * qblocks + qblocks)]
                        kslots += [(c.TOWN + j, 'flag') for j in range(c.TOWN)]
                        nk = len(kslots)
                        for ki, (slot, mode) in enumerate(kslots):
                            sbk = 2 + (ki % 2)
                            b.pguard(sbk)
                            if ki == 0:
                                b.pguard(4)
                                b.pguard(5)
                            T.wait(b.esem(V))
                            T.mm(P[sbk][:, 0:QW],
                                 kT[:, h2, slot * 128:(slot + 1) * 128],
                                 qT[:, h2, qc * QW:qc * QW + QW])
                            b.done(T)
                            S.wait(b.esem(T))
                            S.act(junkV[:, 0:QW], P[sbk][:, 0:QW], AF.Square)
                            b.done(S)
                            b.pmark(sbk, S)
                            V.wait(b.esem(S))
                            b.guard(('Ba2', 0), V)
                            if mode == 'flag':
                                V.stt(junkS[:, 0:QW], junkV[:, 0:QW],
                                      flagc[:, 0:1], junkV[:, 0:QW],
                                      ALU.mult, ALU.mult)
                            else:
                                V.tt(junkS[:, 0:QW], junkV[:, 0:QW],
                                     junkV[:, 0:QW], ALU.mult)
                            if mode == 'own':
                                for i in range(qblocks):
                                    qt = qc * qblocks + i
                                    blkap = junkS[:, i * 128:(i + 1) * 128]
                                    if slot == qt:
                                        V.tt(blkap, blkap, mself[:, :], ALU.mult)
                                    elif slot > qt:
                                        V.memset(blkap, 0.0)
                            b.done(V)
                            T.wait(b.esem(V))
                            T.mm(P[4][:, 0:QW], vbuf[:, slot, h2, :],
                                 junkS[:, 0:QW],
                                 start=(ki == 0), stop=(ki == nk - 1))
                            T.mm(P[5][0:1, 0:QW], onesb[:, 0:1],
                                 junkS[:, 0:QW],
                                 start=(ki == 0), stop=(ki == nk - 1))
                            b.done(T)
                            b.mark(('Ba2', 0), T)
                        V.wait(b.esem(T))
                        b.guard(('rowsc',), V)
                        V.recip(rowsc[0:1, 0:QW], P[5][0:1, 0:QW])
                        b.done(V)
                        b.pmark(5, V)
                        b.pguard(6)
                        T.wait(b.esem(V))
                        T.mm(P[6][:, 0:QW], onesf[0:1, 0:128], rowsc[0:1, 0:QW])
                        b.done(T)
                        b.mark(('rowsc',), T)
                        S.wait(b.esem(T))
                        S.copy(junk2f[:, 0:QW], P[4][:, 0:QW])
                        b.done(S)
                        b.pmark(4, S)
                        V.wait(b.esem(S))
                        V.tt(attn_oT[:, sp * 2 + h2, qc * QW:qc * QW + QW],
                             junk2f[:, 0:QW], P[6][:, 0:QW], ALU.mult)
                        b.done(V)
                        b.pmark(6, V)

            b.cut('pBsp')
            tEs = b.esem(T)
            V.wait(tEs, tEs.n)
            S.wait(tEs, tEs.n)
            stB1.close()
            wob = sb('wob', [128, c.FCH, 512], BF16, stB)
            branch = sb('branchB', [128, c.TOWN, D], BF16, stB)
            for oc in range(D // 512):
                tEs = b.esem(T)
                V.wait(tEs, tEs.n)
                load_cast(wob, dram['w_out'], 0, D, oc * 512, 512)
                for t in range(c.TOWN):
                    pb = t % 2
                    b.pguard(pb)
                    T.wait(b.esem(V))
                    for f in range(c.FCH):
                        T.mm(P[pb][:, 0:512], attn_oT[:, f, t * 128:(t + 1) * 128],
                             wob[:, f, :], start=(f == 0), stop=(f == c.FCH - 1))
                    b.done(T)
                    V.wait(b.esem(T))
                    V.ts(branch[:, t, oc * 512:(oc + 1) * 512], P[pb][:, 0:512],
                         1.0, None, ALU.mult, ALU.add,
                         accum=bsums[:, t * 4 + oc:t * 4 + oc + 1])
                    b.done(V)
                    b.pmark(pb, V)
                    S.wait(b.esem(T))
                    S.act(junkS[:, 0:512], P[pb][:, 0:512], AF.Square,
                          accum=bsqs[:, t * 4 + oc:t * 4 + oc + 1])
                    b.done(S)
                    b.pmarkadd(pb, S)
            V.wait(b.esem(S))
            for t in range(c.TOWN):
                branch_ln_add(branch, t, t, nsum=D // 512)

        b.cut('pB')
        stC = contextlib.ExitStack()
        with stC:
            xqT = sb('xqT', [128, c.FCH, 512], BF16, stC)
            pT = sb('pT', [128, c.NOC, 512], BF16, stC)
            z1t = sb('z1t', [128, c.NOC, 128], BF16, stC)
            ztmp = sb('ztmp', [c.NOC, 128], BF16, stC)
            yq8 = sb('yq8', [128, c.NOC, 512], I8, stC)
            wgq = sb('wgq', [128, c.FCH, 128], BF16, stC)
            wuq = wgq
            branch = bass.AP(xqT, 0, [[c.FCH * 512, 128], [D, 4], [1, D]])
            mxa = junk2f[:, 1024:1152]
            srepg = junk2f[:, 1536:2048]
            c2rep = junk2f[:, 1152:1280].bitcast(F32) if False else junk2f[:, 1152:1280]
            z1s = junkS[:, 1024:1152]
            yqb = junkS[:, 1280:1792]
            wdq = z1t

            def wdqv(oc_i):
                return bass.AP(z1t, oc_i * 512, [[c.NOC * 128, 128], [1, 512]])
            xqtile = junkV
            t1f = junk2f
            NTC = NOWN // 512

            for tc in range(NTC):
                for tt in range(4):
                    tile = tc * 4 + tt
                    tEs = b.esem(T)
                    V.wait(tEs, tEs.n)
                    ln_stats(res[:, tile, :], colB[:, 0:1], colB[:, 1:2], cst[:, 0:1])
                    V.ts(t1f[:, 0:D], res[:, tile, :], colB[:, 0:1], None, ALU.add)
                    V.red(colA[:, 6:7], t1f[:, 0:D], ALU.max, abs_=True)
                    V.ts(colA[:, 7:8], colA[:, 6:7], colB[:, 1:2], op0=ALU.mult)
                    V.ts(colA[:, 7:8], colA[:, 7:8], 1e-5, None, ALU.max)
                    V.recip(colA[:, 8:9], colA[:, 7:8])
                    V.stt(colA[:, 9:10], colA[:, 8:9], 128.0, colB[:, 1:2],
                          ALU.mult, ALU.mult)
                    V.ts(invs[:, tile:tile + 1], colA[:, 7:8], 1.0 / 128.0)
                    V.ts(dusc[:, tile:tile + 1], invs[:, tile:tile + 1],
                         swcols[:, 3:4], op0=ALU.mult)
                    V.ts(t1f[:, 0:D], t1f[:, 0:D], colA[:, 9:10], MAGIC, ALU.mult, ALU.add)
                    V.ts(xqtile[:, 0:D], t1f[:, 0:D], MAGIC, 127.0, ALU.subtract, ALU.min)
                    b.done(V)
                    T.wait(b.esem(V))
                    for f in range(c.FCH):
                        tr128(xqT[:, f, tt * 128:(tt + 1) * 128],
                              xqtile[:, f * 128:(f + 1) * 128], 6 + (f % 2), bf=True)

                for tt in range(4):
                    tile = tc * 4 + tt
                    V.ts(colA[:, 6:7], invs[:, tile:tile + 1], swcols[:, 1:2],
                         op0=ALU.mult)
                    b.done(V)
                    b.pguard(6)
                    T.wait(b.esem(V))
                    T.tr(P[6][:, 0:128], repcol(colA[:, 6:7], 128), identf[:, :])
                    b.done(T)
                    V.wait(b.esem(T))
                    b.guard(('rowsc',), V)
                    V.copy(rowsc[0:1, 8 + tt * 128:8 + (tt + 1) * 128],
                           P[6][0:1, 0:128])
                    b.done(V)
                    b.pmark(6, V)
                b.pguard(6)
                T.wait(b.esem(V))
                T.mm(P[6][:, 0:512], onesf[0:1, 0:128], rowsc[0:1, 8:8 + 512])
                b.done(T)
                b.mark(('rowsc',), T)
                V.wait(b.esem(T))
                V.copy(srepg, P[6][:, 0:512])
                b.done(V)
                b.pmark(6, V)

                for o in range(c.NOC):
                    tEs = b.esem(T)
                    V.wait(tEs, tEs.n)
                    S.wait(tEs, tEs.n)
                    load_cast(wgq, dram['wgT'], 0, D, o * 128, 128, quant='tern', swi=0)
                    b.pguard(0)
                    b.pguard(1)
                    T.wait(b.esem(V))
                    for f in range(c.FCH):
                        T.mm(P[0][:, 0:512], wgq[:, f, :], xqT[:, f, 0:512],
                             start=(f == 0), stop=(f == c.FCH - 1))
                    b.done(T)
                    tEs2 = b.esem(T)
                    V.wait(tEs2, tEs2.n)
                    S.wait(tEs2, tEs2.n)
                    load_cast(wuq, dram['wuT'], 0, D, o * 128, 128, quant='tern', swi=2)
                    T.wait(b.esem(V))
                    for f in range(c.FCH):
                        T.mm(P[1][:, 0:512], wuq[:, f, :], xqT[:, f, 0:512],
                             start=(f == 0), stop=(f == c.FCH - 1))
                    b.done(T)
                    V.wait(b.esem(T))
                    V.tt(junk2f[:, 0:512], P[0][:, 0:512], srepg, ALU.mult)
                    b.done(V)
                    b.pmark(0, V)
                    S.wait(b.esem(V))
                    S.act(junk2f[:, 0:512], junk2f[:, 0:512], AF.Silu)
                    b.done(S)
                    V.wait(b.esem(S))
                    V.tt(pT[:, o, :], junk2f[:, 0:512], P[1][:, 0:512], ALU.mult)
                    b.done(V)
                    b.pmark(1, V)

                for tb in range(4):
                    tile = tc * 4 + tb
                    for j in range(c.NOC):
                        pbk = j % 2
                        b.pguard(pbk)
                        T.wait(b.esem(V))
                        T.mm(P[pbk][:, 0:128], h128[:, :],
                             pT[:, j, tb * 128:(tb + 1) * 128])
                        b.done(T)
                        V.wait(b.esem(T))
                        V.copy(z1s, P[pbk][:, 0:128])
                        b.done(V)
                        b.pmark(pbk, V)
                        T.wait(b.esem(V))
                        tr128(z1t[:, j, :], z1s, 6 + (j % 2), bf=True)
                    T.wait(b.esem(V))
                    for ilo in range(128):
                        bank = 6 + (ilo % 2)
                        vA = bass.AP(z1t, ilo, [[c.NOC * 128, 128], [128, c.NOC]])
                        b.pguard(bank)
                        T.tr(P[bank][0:128, 0:64].bitcast(BF16)[0:c.NOC, 0:128],
                             vA, identb[:, :])
                        b.done(T)
                        V.wait(b.esem(T))
                        V.copy(ztmp[:, :],
                               P[bank][0:128, 0:64].bitcast(BF16)[0:c.NOC, 0:128])
                        b.done(V)
                        b.pmark(bank, V)
                        pbk = 2 + (ilo % 2)
                        b.pguard(pbk)
                        T.wait(b.esem(V))
                        T.mm(P[pbk][0:c.NOC, 0:128], hj[:, :], ztmp[:, :])
                        b.done(T)
                        d, q = ilo // c.G, ilo % c.G
                        V.wait(b.esem(T))
                        V.copy(pT[q * c.NOC:(q + 1) * c.NOC, d,
                                  tb * 128:(tb + 1) * 128],
                               P[pbk][0:c.NOC, 0:128])
                        b.done(V)
                        b.pmark(pbk, V)
                    vw = bass.AP(pT, tb * 128, [[c.NOC * 512, 128], [1, 128],
                                                [512, c.NOC]])
                    V.red(mxa, vw, ALU.max, abs_=True)
                    b.done(V)
                    b.pguard(6)
                    T.wait(b.esem(V))
                    T.tr(P[6][:, 0:128], mxa, identf[:, :])
                    b.done(T)
                    V.wait(b.esem(T))
                    V.red(colA[:, 6:7], P[6][:, 0:128], ALU.max)
                    V.ts(colA[:, 7:8], colA[:, 6:7], dusc[:, tile:tile + 1],
                         op0=ALU.mult)
                    V.ts(colA[:, 7:8], colA[:, 7:8], 1e-5, None, ALU.max)
                    V.recip(colA[:, 8:9], colA[:, 7:8])
                    V.stt(colA[:, 9:10], colA[:, 8:9], 128.0,
                          dusc[:, tile:tile + 1], ALU.mult, ALU.mult)
                    V.stt(dsc[:, tile:tile + 1], colA[:, 7:8], 1.0 / 128.0,
                          swcols[:, 5:6], ALU.mult, ALU.mult)
                    b.done(V)
                    b.pmark(6, V)
                    b.pguard(6)
                    T.wait(b.esem(V))
                    T.tr(P[6][:, 0:128], repcol(colA[:, 9:10], 128), identf[:, :])
                    b.done(T)
                    V.wait(b.esem(T))
                    b.guard(('rowsc',), V)
                    V.copy(rowsc[0:1, 8:8 + 128], P[6][0:1, 0:128])
                    b.done(V)
                    b.pmark(6, V)
                    b.pguard(6)
                    T.wait(b.esem(V))
                    T.mm(P[6][:, 0:128], onesf[0:1, 0:128], rowsc[0:1, 8:8 + 128])
                    b.done(T)
                    b.mark(('rowsc',), T)
                    V.wait(b.esem(T))
                    V.copy(c2rep, P[6][:, 0:128])
                    b.done(V)
                    b.pmark(6, V)
                    for j in range(c.NOC):
                        V.tt(junk2f[:, 0:128], pT[:, j, tb * 128:(tb + 1) * 128],
                             c2rep, ALU.mult)
                        V.ts(junk2f[:, 0:128], junk2f[:, 0:128], MAGIC, None,
                             ALU.add)
                        V.ts(yq8[:, j, tb * 128:(tb + 1) * 128], junk2f[:, 0:128],
                             MAGIC, 127.0, ALU.subtract, ALU.min)
                        b.done(V)

                for op2 in range(max(1, D // 1024)):
                    for k in range(c.NOC):
                        for oc_i in range(min(2, D // 512)):
                            oc = op2 * 2 + oc_i
                            par = b.hist.get('_wq', 0) % 2
                            b.hist['_wq'] = b.hist.get('_wq', 0) + 1
                            b.guard(('wscr', par), Y)
                            Y.dma(wf32[:, par, 0:512],
                                  dram['wdTp'][k * 128:(k + 1) * 128,
                                               oc * 512:(oc + 1) * 512])
                            b.dinc(Y)
                            b.dwait(S)
                            tEs = b.esem(T)
                            S.wait(tEs, tEs.n)
                            V.wait(tEs, tEs.n)
                            S.act(junkS[:, 0:512], wf32[:, par, 0:512], AF.Sign,
                                  bias=swcols[:, 4:5])
                            S.act(wdqv(oc_i), wf32[:, par, 0:512], AF.Sign,
                                  bias=swcols[:, 5:6])
                            b.done(S)
                            V.wait(b.esem(S))
                            V.tt(wdqv(oc_i), wdqv(oc_i), junkS[:, 0:512],
                                 ALU.add)
                            b.done(V)
                            b.mark(('wscr', par), V)
                        for tb in range(4):
                            V.copy(yqb[:, tb * 128:(tb + 1) * 128],
                                   yq8[:, k, tb * 128:(tb + 1) * 128])
                        b.done(V)
                        if k == 0:
                            for bk in range(8):
                                b.pguard(bk)
                        T.wait(b.esem(V))
                        for oc_i in range(min(2, D // 512)):
                            for tb in range(4):
                                T.mm(P[oc_i * 4 + tb][:, 0:512],
                                     yqb[:, tb * 128:(tb + 1) * 128],
                                     wdqv(oc_i), start=(k == 0),
                                     stop=(k == c.NOC - 1))
                        b.done(T)
                    for oc_i in range(min(2, D // 512)):
                        oc = op2 * 2 + oc_i
                        for tb in range(4):
                            tile = tc * 4 + tb
                            bk = oc_i * 4 + tb
                            V.wait(b.esem(T))
                            V.ts(branch[:, tb, oc * 512:(oc + 1) * 512],
                                 P[bk][:, 0:512], dsc[:, tile:tile + 1],
                                 None, ALU.mult, ALU.add,
                                 accum=bsums[:, tile * 4 + oc:tile * 4 + oc + 1])
                            b.done(V)
                            b.pmark(bk, V)
                            S.wait(b.esem(T))
                            S.act(junkS[:, 0:512], P[bk][:, 0:512],
                                  AF.Square, scale=dsc[:, tile:tile + 1],
                                  accum=bsqs[:, tile * 4 + oc:tile * 4 + oc + 1])
                            b.done(S)
                            b.pmarkadd(bk, S)
                V.wait(b.esem(S))
                for tb in range(4):
                    branch_ln_add(branch, tb, tc * 4 + tb, nsum=D // 512)

        b.cut('pC')
        stD = contextlib.ExitStack()
        with stD:
            NCT, NH, ST, CH, HD = c.NCT, c.NH, c.ST, c.CH, c.L1HD
            NST, NG, HPG = c.NST, c.NG, c.HPG
            DCT = D // 128
            stD1 = contextlib.ExitStack()
            stD2 = contextlib.ExitStack()
            HB = 64
            dtvb = sb('dtvb', [128, NOWN], BF16, stD)
            acs = sb('acs', [128, NOWN], F32, stD)
            expb = sb('expb', [128, NOWN], BF16, stD)
            decb = sb('decb', [128, NOWN], BF16, stD)
            hdsel = sb('hdsel', [128, NH * HD], BF16, stD)
            hdsel32 = sb('hdsel32', [128, c.NCH * NH], F32, stD)
            dec_all = sb('dec_all', [128, c.NCH * NH], F32, stD)
            cd_all = sb('cd_all', [128, c.NCH * NH], F32, stD)
            cdk = sb('cdk', [128, c.NCH], F32, stD)
            cwcol = sb('cwcol', [128, NCT, c.KCONV], F32, stD)
            cbcol = sb('cbcol', [128, NCT], F32, stD)
            dtbcol = sb('dtbcol', [128, 1], F32, stD)
            acol = sb('acol', [128, 1], F32, stD)
            negf = sb('negf', [128, 64], F32, stD)
            cbtf = sb('cbtf', [64, 64], F32, stD)
            tmpd = sb('tmpd', [128, D], F32, stD)
            acsr0 = tmpd[0:1, 0:c.NH * c.CH]
            ysl2 = junk2f[0:64, 0:512]
            segf = junk2f[0:64, 1280:1344]
            mhtb = junk2f[64:128, 512:544].bitcast(BF16)
            ysl = junkV[0:64, 0:1024].bitcast(F32)
            xs_tm = junkV[64:128, 0:D]
            xt_tm = junkS[64:128, 0:D]
            xtd_tm = junk2f[64:128, 1024:2048].bitcast(BF16)
            bt_tm = junk2f[64:128, 576:832].bitcast(BF16)
            xbcT = sb('xbcT', [128, NCT, NOWN + HB], BF16, stD2, side='right')
            dtT = sb('dtT', [128, NOWN], BF16, stD2, side='right')
            dtv = sb('dtv', [128, NOWN], F32, stD2, side='right')
            ccbuf = sb('ccbuf', [128, 4 * NCT], F32, stD2, side='right')
            xn1T = sb('xn1T', [128, c.FCH, NOWN], BF16, stD1, side='right')
            wsl2a = sb('wsl2a', [128, c.FCH, 512], BF16, stD1, side='right')
            wsl2 = wsl2a[:, :, 0:128]
            zstage = junkV[:, 0:512]

            def tr64(dst, src, bank):
                b.pguard(bank)
                T.tr(pbf64(bank), src, identb[:, :])
                b.done(T)
                V.wait(b.esem(T))
                V.copy(dst, pbf64(bank))
                b.done(V)
                b.pmark(bank, V)

            V.memset(negf[:, :], -1.0)
            V.memset(hdsel[0:NH, :], 1.0)
            b.done(V)
            G.wait(b.esem(V))
            G.affsel(hdsel[0:NH, :], hdsel[0:NH, :], [[-1, NH], [0, HD]],
                     ALU.is_equal)
            b.done(G)
            Y.dma(cwcol[:, :, :],
                  bass.AP(dram['conv_w'].tensor, 0,
                          [[c.KCONV, 128], [128 * c.KCONV, NCT], [1, c.KCONV]]))
            b.dinc(Y)
            Y.dma(cbcol[:, :],
                  bass.AP(dram['conv_b'].tensor, 0, [[1, 128], [128, NCT]]),
                  slow=True)
            b.dinc(Y)
            for srcnm, dstc in (('dt_bias', dtbcol), ('A_log', acol)):
                b.guard(('rowsc',), Y)
                Y.dma(rowsc[0:1, 0:NH], dram[srcnm])
                b.dinc(Y)
                b.pguard(6)
                b.dwait(T)
                T.tr(P[6][0:NH, 0:1], rowsc[0:1, 0:NH], onesf[0:1, 0:1])
                b.done(T)
                b.mark(('rowsc',), T)
                V.wait(b.esem(T))
                V.copy(dstc[0:NH, 0:1], P[6][0:NH, 0:1])
                b.done(V)
                b.pmark(6, V)
            S.wait(b.esem(V))
            S.act(acol[0:NH, 0:1], acol[0:NH, 0:1], AF.Exp)
            b.done(S)
            V.wait(b.esem(S))
            V.ts(acol[0:NH, 0:1], acol[0:NH, 0:1], -1.0)
            b.done(V)
            for oc in range(D // 512):
                row_to_allparts(tmpd[:, oc * 512:(oc + 1) * 512],
                                dram['norm_w'][0:1, oc * 512:(oc + 1) * 512], 512)
            for t in range(c.TOWN):
                S.wait(b.esem(V))
                S.act(junkS[:, 0:D], res[:, t, :], AF.Square, accum=colA[:, 0:1])
                b.done(S)
                V.wait(b.esem(S))
                V.ts(colA[:, 1:2], colA[:, 0:1], 1.0 / D)
                b.done(V)
                S.wait(b.esem(V))
                S.act(colA[:, 2:3], colA[:, 1:2], AF.Sqrt, bias=cst[:, 0:1])
                b.done(S)
                V.wait(b.esem(S))
                V.recip(colA[:, 3:4], colA[:, 2:3])
                V.ts(junk2f[:, 0:D], res[:, t, :], colA[:, 3:4], None, ALU.mult)
                V.tt(junk2f[:, 0:D], junk2f[:, 0:D], tmpd[:, 0:D], ALU.mult)
                b.done(V)
                T.wait(b.esem(V))
                for f in range(c.FCH):
                    tr128(xn1T[:, f, t * 128:(t + 1) * 128],
                          junk2f[:, f * 128:(f + 1) * 128], 6 + (f % 2))

            b.cut('pD1')
            zsp = zspill.ap()
            for oc in range(D // 512):
                tEs = b.esem(T)
                V.wait(tEs, tEs.n)
                load_cast(wsl2a, dram['w_in'], 0, D, oc * 512, 512)
                for t in range(c.TOWN):
                    pb = t % 2
                    b.pguard(pb)
                    T.wait(b.esem(V))
                    for f in range(c.FCH):
                        T.mm(P[pb][:, 0:512], xn1T[:, f, t * 128:(t + 1) * 128],
                             wsl2a[:, f, :], start=(f == 0), stop=(f == c.FCH - 1))
                    b.done(T)
                    S.wait(b.esem(T))
                    b.guard(('zst',), S)
                    S.act(zstage[:, 0:512], P[pb][:, 0:512], AF.Silu)
                    b.done(S)
                    b.pmark(pb, S)
                    Y.wait(b.esem(S))
                    Y.dma(zsp[t * 128:(t + 1) * 128, oc * 512:(oc + 1) * 512],
                          zstage[:, 0:512])
                    b.dinc(Y)
                    Y.wait(dD)
                    b.hist[('zst',)] = [(dD, dD.n)]

            b.cut('pD2')
            nq = NOWN // 512
            for ct in range(NCT + 1):
                wid = 128 if ct < NCT else NH
                tEs = b.esem(T)
                V.wait(tEs, tEs.n)
                load_cast(wsl2, dram['w_in'], 0, D, D + ct * 128, wid)
                for tq in range(nq):
                    pb = tq % 2
                    b.pguard(pb)
                    T.wait(b.esem(V))
                    for f in range(c.FCH):
                        T.mm(P[pb][0:wid, 0:512], wsl2[:, f, 0:wid],
                             xn1T[:, f, tq * 512:(tq + 1) * 512],
                             start=(f == 0), stop=(f == c.FCH - 1))
                    b.done(T)
                    V.wait(b.esem(T))
                    if ct < NCT:
                        V.copy(xbcT[:, ct, HB + tq * 512:HB + (tq + 1) * 512],
                               P[pb][:, 0:512])
                    else:
                        V.copy(dtT[0:NH, tq * 512:(tq + 1) * 512],
                               P[pb][0:NH, 0:512])
                    b.done(V)
                    b.pmark(pb, V)
            tEs = b.esem(T)
            V.wait(tEs, tEs.n)
            S.wait(tEs, tEs.n)
            stD1.close()

            b.cut('pD3')
            V.memset(ccbuf[:, :], 0.0)
            b.done(V)
            for ct in range(NCT):
                V.copy(ccbuf[:, ct * 4:ct * 4 + 3],
                       xbcT[:, ct, HB + NOWN - 3:HB + NOWN])
                b.done(V)
            Y.wait(b.esem(V))
            for ct in range(NCT):
                Y.dma(bass.AP(cc1_in, ct * 128, [[1, 128], [c.CONVD, 4]]),
                      ccbuf[:, ct * 4:(ct + 1) * 4], slow=True)
                b.dinc(Y)
            G.wait(dD)
            G.cc('AllGather', ALU.bypass, c.groups,
                 [cc1_in.ap().opt()], [cc1_out.ap().opt()])
            ccsem.n += 1
            G.last.then_inc(ccsem.h, 1)
            Y.wait(ccsem)
            for ct in range(NCT):
                Y.dma(ccbuf[:, ct * 4:(ct + 1) * 4],
                      bass.AP(cc1_out, ct * 128, [[1, 128], [c.CONVD, 4]]),
                      slow=True)
                b.dinc(Y)
            b.dwait(V)
            for ct in range(NCT):
                V.ts(junk2f[:, 0:3], ccbuf[:, ct * 4:ct * 4 + 3], flagc[:, 0:1],
                     None, ALU.mult)
                V.copy(xbcT[:, ct, HB - 3:HB], junk2f[:, 0:3])
                b.done(V)

            b.cut('pCC1')
            stCV = contextlib.ExitStack()
            convT = sb('convT', [128, NCT, NOWN], BF16, stCV)
            for ct in range(NCT):
                V.wait(b.esem(S))
                V.stt(tmpd[:, 0:NOWN], xbcT[:, ct, HB - 3:HB - 3 + NOWN],
                      cwcol[:, ct, 0:1], xbcT[:, ct, HB - 2:HB - 2 + NOWN],
                      ALU.mult, ALU.bypass)
                for kk in range(1, c.KCONV):
                    V.stt(tmpd[:, 0:NOWN],
                          xbcT[:, ct, HB - 3 + kk:HB - 3 + kk + NOWN],
                          cwcol[:, ct, kk:kk + 1], tmpd[:, 0:NOWN],
                          ALU.mult, ALU.add)
                b.done(V)
                S.wait(b.esem(V))
                S.act(convT[:, ct, :], tmpd[:, 0:NOWN], AF.Silu,
                      bias=cbcol[:, ct:ct + 1])
                b.done(S)

            b.cut('d4a')
            S.act(dtv[0:NH, :], dtT[0:NH, :], AF.Exp, bias=dtbcol[0:NH, 0:1])
            S.act(dtv[0:NH, :], dtv[0:NH, :], AF.Ln, bias=cst[0:NH, 2:3])
            b.done(S)
            b.cut('d4b1')
            V.wait(b.esem(S))
            V.ts(acs[0:NH, :], dtv[0:NH, :], acol[0:NH, 0:1], None, ALU.mult)
            V.copy(dtvb[0:NH, :], dtv[0:NH, :])
            b.done(V)
            for ch in range(c.NCH):
                V.scan(acs[0:NH, ch * CH:(ch + 1) * CH],
                       acs[0:NH, ch * CH:(ch + 1) * CH],
                       repcol(cst[0:NH, 3:4], CH), 0.0, ALU.add, ALU.add)
                b.done(V)
            b.cut('d4scan')
            S.wait(b.esem(V))
            S.act(expb[0:NH, :], acs[0:NH, :], AF.Exp)
            for ch in range(c.NCH):
                S.act(decb[0:NH, ch * CH:(ch + 1) * CH],
                      acs[0:NH, ch * CH:(ch + 1) * CH], AF.Exp, scale=-1.0,
                      bias=bass.AP(acs, (ch + 1) * CH - 1, [[NOWN, NH], [0, 1]]))
            b.done(S)
            b.cut('d4b3')
            V.memset(hdsel32[0:NH, :], 1.0)
            b.done(V)
            G.wait(b.esem(V))
            G.affsel(hdsel32[0:NH, :], hdsel32[0:NH, :], [[0, c.NCH], [-1, NH]],
                     ALU.is_equal)
            b.done(G)
            b.cut('d4b4pre')
            S.wait(b.esem(V))
            S.act(cdk[0:NH, :], bass.AP(acs, CH - 1, [[NOWN, NH], [CH, c.NCH]]),
                  AF.Exp)
            b.done(S)
            b.cut('d4b4')
            V.wait(b.esem(S))
            V.wait(b.esem(G))
            V.tt(hdsel32[0:NH, :], hdsel32[0:NH, :],
                 bass.AP(cdk, 0, [[c.NCH, NH], [1, c.NCH], [0, NH]]), ALU.mult)
            b.done(V)
            b.pguard(6)
            T.wait(b.esem(V))
            T.mm(P[6][:, 0:c.NCH * NH], onesf[0:NH, 0:128], hdsel32[0:NH, :])
            b.done(T)
            V.wait(b.esem(T))
            V.copy(dec_all[:, :], P[6][:, 0:c.NCH * NH])
            b.done(V)
            b.pmark(6, V)
            V.memset(cd_all[:, 0:NH], 1.0)
            for ch in range(1, c.NCH):
                V.tt(cd_all[:, ch * NH:(ch + 1) * NH],
                     cd_all[:, (ch - 1) * NH:ch * NH],
                     dec_all[:, (ch - 1) * NH:ch * NH], ALU.mult)
            b.done(V)
            stD2.close()

            b.cut('pD4')
            stYB = contextlib.ExitStack()
            ybuf = sb('ybuf', [128, c.TOWN, D], BF16, stYB, side='right')
            stS = contextlib.ExitStack()
            Sst = sb('Sst', [128, NST, NH * HD], F32, stS)
            Sstb = sb('Sstb', [128, NST, NH * HD], BF16, stS)
            dskip_tm = junkS[0:64, 0:NH * HD]
            b.guard(('rowsc',), Y)
            Y.dma(rowsc[0:1, 0:NH], dram['D_skip'])
            b.dinc(Y)
            b.pguard(6)
            b.dwait(T)
            T.tr(P[6][0:NH, 0:1], rowsc[0:1, 0:NH], onesf[0:1, 0:1])
            b.done(T)
            b.mark(('rowsc',), T)
            V.wait(b.esem(T))
            V.copy(colB[0:NH, 4:5], P[6][0:NH, 0:1])
            b.done(V)
            b.pmark(6, V)
            V.copy(junk2f[0:NH, 0:64], repcol(colB[0:NH, 4:5], 64))
            V.copy(junkV[0:NH, 0:64], junk2f[0:NH, 0:64])
            b.done(V)
            for g in range(NG):
                b.pguard(6)
                T.wait(b.esem(V))
                T.mm(P[6][0:64, 0:512], junkV[0:NH, 0:64],
                     hdsel[0:NH, g * 512:(g + 1) * 512])
                b.done(T)
                V.wait(b.esem(T))
                V.copy(dskip_tm[:, g * 512:(g + 1) * 512], P[6][0:64, 0:512])
                b.done(V)
                b.pmark(6, V)
            V.memset(Sst[:, :, :], 0.0)
            V.memset(Sstb[:, :, :], 0.0)
            b.done(V)
            for ch in range(c.NCH):
                t0 = ch * CH
                half = ch % 2
                T.wait(b.esem(V))
                for ct in range(DCT):
                    tr64(xs_tm[:, ct * 128:(ct + 1) * 128],
                         convT[:, ct, t0:t0 + CH], 6 + (ct % 2))
                for i2 in range(NST):
                    tr64(bt_tm[:, i2 * 128:(i2 + 1) * 128],
                         convT[:, DCT + i2, t0:t0 + CH], 6 + (i2 % 2))
                b.pguard(2)
                T.wait(b.esem(V))
                for i2 in range(NST):
                    T.mm(P[2][0:64, 0:64], convT[:, DCT + i2, t0:t0 + CH],
                         convT[:, DCT + NST + i2, t0:t0 + CH],
                         start=(i2 == 0), stop=(i2 == NST - 1))
                b.done(T)
                V.wait(b.esem(T))
                V.copy(cbtf[:, :], P[2][0:64, 0:64])
                b.done(V)
                b.pmark(2, V)
                G.wait(b.esem(V))
                G.tt(cbtf[:, :], cbtf[:, :], mself[0:64, 0:64], ALU.mult)
                b.done(G)
                V.wait(b.esem(G))
                b.guard(('acsr',), Y)
                Y.wait(b.esem(V))
                Y.dma(bass.AP(tmpd, 0, [[NH * CH, 1], [CH, NH], [1, CH]]),
                      acs[0:NH, t0:t0 + CH])
                b.dinc(Y)
                b.dwait(T)
                for g in range(NG):
                    gsl = slice(g * 512, (g + 1) * 512)
                    b.pguard(2)
                    b.pguard(3)
                    b.pguard(6)
                    T.wait(b.esem(V))
                    T.mm(P[2][0:64, 0:512], dtvb[0:NH, t0:t0 + CH], hdsel[0:NH, gsl])
                    T.mm(P[3][0:64, 0:512], decb[0:NH, t0:t0 + CH], hdsel[0:NH, gsl])
                    T.mm(P[6][0:64, 0:512], expb[0:NH, t0:t0 + CH], hdsel[0:NH, gsl])
                    b.done(T)
                    V.wait(b.esem(T))
                    V.tt(xt_tm[:, gsl], xs_tm[:, gsl], P[2][0:64, 0:512], ALU.mult)
                    V.tt(xtd_tm[:, gsl], xt_tm[:, gsl], P[3][0:64, 0:512], ALU.mult)
                    b.done(V)
                    b.pmark(2, V)
                    b.pmark(3, V)
                    b.pguard(4)
                    b.pguard(5)
                    for hh in range(HPG):
                        h = g * HPG + hh
                        pbk = 2 + (hh % 2)
                        b.pguard(pbk)
                        T.mm(P[pbk][0:64, 0:64],
                             acsr0[0:1, h * CH:(h + 1) * CH],
                             negf[0:1, 0:64], start=True, stop=False)
                        T.mm(P[pbk][0:64, 0:64], onesf[0:1, 0:64],
                             acsr0[0:1, h * CH:(h + 1) * CH],
                             start=False, stop=True)
                        b.done(T)
                        b.mark(('acsr',), T)
                        V.wait(b.esem(T))
                        V.ts(segf, P[pbk][0:64, 0:64], 0.0, None, ALU.min)
                        b.done(V)
                        b.pmark(pbk, V)
                        S.wait(b.esem(V))
                        S.act(segf, segf, AF.Exp)
                        b.done(S)
                        V.wait(b.esem(S))
                        V.tt(mhtb, segf, cbtf[:, :], ALU.mult)
                        b.done(V)
                        T.wait(b.esem(V))
                        T.mm(P[4][0:64, hh * 64:(hh + 1) * 64], mhtb,
                             xt_tm[:, h * 64:(h + 1) * 64])
                        for i2 in range(NST):
                            T.mm(P[5][0:64, hh * 64:(hh + 1) * 64],
                                 convT[:, DCT + NST + i2, t0:t0 + CH],
                                 Sstb[:, i2, h * 64:(h + 1) * 64],
                                 start=(i2 == 0), stop=(i2 == NST - 1))
                        b.done(T)
                    V.wait(b.esem(T))
                    V.copy(ysl, P[6][0:64, 0:512])
                    V.tt(ysl, ysl, P[5][0:64, 0:512], ALU.mult)
                    V.stt(ysl, ysl, 1.0, P[4][0:64, 0:512],
                          ALU.mult, ALU.add)
                    V.copy(ysl2, xs_tm[:, gsl])
                    V.tt(ysl2, ysl2, dskip_tm[:, gsl], ALU.mult)
                    V.tt(ysl, ysl, ysl2, ALU.add)
                    V.copy(ybuf[:, ch // 2, gsl][64 * half:64 * half + 64, :],
                           ysl)
                    b.done(V)
                    b.pmark(4, V)
                    b.pmark(5, V)
                    b.pmark(6, V)
                for g in range(NG):
                    gsl = slice(g * 512, (g + 1) * 512)
                    for i2 in range(NST):
                        pbk = 2 + ((g + i2) % 2)
                        b.pguard(pbk)
                        T.wait(b.esem(V))
                        T.mm(P[pbk][0:128, 0:512],
                             bt_tm[:, i2 * 128:(i2 + 1) * 128], xtd_tm[:, gsl])
                        b.done(T)
                        decv = bass.AP(dec_all, ch * NH + g * HPG,
                                       [[c.NCH * NH, 128], [1, HPG], [0, 64]])
                        V.wait(b.esem(T))
                        V.tt(Sst[:, i2, gsl], Sst[:, i2, gsl], decv, ALU.mult)
                        V.tt(Sst[:, i2, gsl], Sst[:, i2, gsl],
                             P[pbk][0:128, 0:512], ALU.add)
                        V.copy(Sstb[:, i2, gsl], Sst[:, i2, gsl])
                        b.done(V)
                        b.pmark(pbk, V)

            b.cut('pD5')
            Y.wait(b.esem(V))
            Y.dma(cc2_in.ap().rearrange('(i p) w -> p i w', p=128), Sst[:, :, :])
            b.dinc(Y)
            G.wait(dD)
            G.cc('AllGather', ALU.bypass, c.groups,
                 [cc2_in.ap().opt()], [cc2_out.ap().opt()])
            ccsem.n += 1
            G.last.then_inc(ccsem.h, 1)
            Y.wait(ccsem)
            stD3 = contextlib.ExitStack()
            Sinib = sb('Sinib', [128, NST, NH * HD], BF16, stD3, side='right')
            for i2 in range(NST):
                Y.dma(tmpd[:, 0:NH * HD],
                      cc2_out.ap().rearrange('(i p) w -> p i w', p=128)[:, i2, :])
                b.dinc(Y)
                b.dwait(V)
                V.ts(junk2f[:, 0:NH * HD], tmpd[:, 0:NH * HD], flagc[:, 0:1],
                     None, ALU.mult)
                V.copy(Sinib[:, i2, :], junk2f[:, 0:NH * HD])
                b.done(V)
                Y.wait(b.esem(V))

            b.cut('pCC2')
            for ch in range(c.NCH):
                t0 = ch * CH
                half = ch % 2
                for g in range(NG):
                    gsl = slice(g * 512, (g + 1) * 512)
                    b.pguard(4)
                    b.pguard(6)
                    T.wait(b.esem(V))
                    for hh in range(HPG):
                        h = g * HPG + hh
                        for i2 in range(NST):
                            T.mm(P[4][0:64, hh * 64:(hh + 1) * 64],
                                 convT[:, DCT + NST + i2, t0:t0 + CH],
                                 Sinib[:, i2, h * 64:(h + 1) * 64],
                                 start=(i2 == 0), stop=(i2 == NST - 1))
                    T.mm(P[6][0:64, 0:512], expb[0:NH, t0:t0 + CH], hdsel[0:NH, gsl])
                    b.done(T)
                    V.wait(b.esem(T))
                    cdv = bass.AP(cd_all, ch * NH + g * HPG,
                                  [[c.NCH * NH, 64], [1, HPG], [0, 64]])
                    V.tt(ysl, P[6][0:64, 0:512], cdv, ALU.mult)
                    V.tt(ysl, ysl, P[4][0:64, 0:512], ALU.mult)
                    bsl = ybuf[:, ch // 2, gsl][64 * half:64 * half + 64, :]
                    if half == 0:
                        V.tt(bsl, bsl, ysl, ALU.add)
                    else:
                        sc64b = junk2f[64:128, 0:512]
                        V.copy(sc64b, ysl)
                        V.tt(bsl, bsl, sc64b, ALU.add)
                    b.done(V)
                    b.pmark(4, V)
                    b.pmark(6, V)
            stD3.close()
            tEs = b.esem(T)
            V.wait(tEs, tEs.n)
            S.wait(tEs, tEs.n)
            stS.close()
            stCV.close()

            b.cut('pD5c')
            stD4 = contextlib.ExitStack()
            tmpT = sb('tmpT', [128, c.FCH, NOWN], BF16, stD4)
            stD4b = contextlib.ExitStack()
            ynT = sb('ynT', [128, c.FCH, NOWN], BF16, stD4b)
            wsl2b = sb('wsl2b', [128, c.FCH, 128], BF16, stD4b)
            for oc in range(D // 512):
                row_to_allparts(tmpd[:, oc * 512:(oc + 1) * 512],
                                dram['gnorm_w'][0:1, oc * 512:(oc + 1) * 512], 512)
            for t in range(c.TOWN):
                b.guard(('zld',), Y)
                Y.dma(junkV[:, 0:D], zsp[t * 128:(t + 1) * 128, :])
                b.dinc(Y)
                b.dwait(V)
                V.wait(b.esem(S))
                V.tt(junk2f[:, 0:D], ybuf[:, t, :], junkV[:, 0:D], ALU.mult)
                b.done(V)
                b.mark(('zld',), V)
                S.wait(b.esem(V))
                S.act(junkS[:, 0:D], junk2f[:, 0:D], AF.Square, accum=colA[:, 0:1])
                b.done(S)
                V.wait(b.esem(S))
                V.ts(colA[:, 1:2], colA[:, 0:1], 1.0 / D)
                b.done(V)
                S.wait(b.esem(V))
                S.act(colA[:, 2:3], colA[:, 1:2], AF.Sqrt, bias=cst[:, 0:1])
                b.done(S)
                V.wait(b.esem(S))
                V.recip(colA[:, 3:4], colA[:, 2:3])
                V.ts(junk2f[:, 0:D], junk2f[:, 0:D], colA[:, 3:4], None, ALU.mult)
                V.tt(junk2f[:, 0:D], junk2f[:, 0:D], tmpd[:, 0:D], ALU.mult)
                b.done(V)
                T.wait(b.esem(V))
                for f in range(c.FCH):
                    tr128(ynT[:, f, t * 128:(t + 1) * 128],
                          junk2f[:, f * 128:(f + 1) * 128], 6 + (f % 2))
            for ot in range(c.FCH):
                tEs = b.esem(T)
                V.wait(tEs, tEs.n)
                load_cast(wsl2b, dram['w_out1'], 0, D, ot * 128, 128)
                for tq in range(NOWN // 512):
                    pb = tq % 2
                    b.pguard(pb)
                    T.wait(b.esem(V))
                    for f in range(c.FCH):
                        T.mm(P[pb][:, 0:512], wsl2b[:, f, 0:128],
                             ynT[:, f, tq * 512:(tq + 1) * 512],
                             start=(f == 0), stop=(f == c.FCH - 1))
                    b.done(T)
                    V.wait(b.esem(T))
                    V.copy(tmpT[:, ot, tq * 512:(tq + 1) * 512], P[pb][:, 0:512])
                    b.done(V)
                    b.pmark(pb, V)
            tEs = b.esem(T)
            V.wait(tEs, tEs.n)
            S.wait(tEs, tEs.n)
            stD4b.close()
            wsl1 = sb('wsl1', [128, c.FCH, 512], BF16, stD4)
            branch = ybuf
            for oc in range(D // 512):
                tEs = b.esem(T)
                V.wait(tEs, tEs.n)
                load_cast(wsl1, dram['wo_w'], 0, D, oc * 512, 512)
                for t in range(c.TOWN):
                    pb = t % 2
                    b.pguard(pb)
                    T.wait(b.esem(V))
                    for f in range(c.FCH):
                        T.mm(P[pb][:, 0:512], tmpT[:, f, t * 128:(t + 1) * 128],
                             wsl1[:, f, :], start=(f == 0), stop=(f == c.FCH - 1))
                    b.done(T)
                    V.wait(b.esem(T))
                    V.ts(branch[:, t, oc * 512:(oc + 1) * 512], P[pb][:, 0:512],
                         1.0, None, ALU.mult, ALU.add,
                         accum=bsums[:, t * 4 + oc:t * 4 + oc + 1])
                    b.done(V)
                    b.pmark(pb, V)
                    S.wait(b.esem(T))
                    S.act(junkS[:, 0:512], P[pb][:, 0:512], AF.Square,
                          accum=bsqs[:, t * 4 + oc:t * 4 + oc + 1])
                    b.done(S)
                    b.pmarkadd(pb, S)
            V.wait(b.esem(S))
            for t in range(c.TOWN):
                branch_ln_add(branch, t, t, nsum=D // 512)
            stD4.close()
            stYB.close()

        b.cut('pD')
        stE = contextlib.ExitStack()
        with stE:
            xqTe = sb('xqTe', [128, c.FCH, 512], BF16, stE)
            pTe = sb('pTe', [128, c.NOC, 512], BF16, stE)
            wgqe = sb('wgqe', [128, c.FCH, 128], BF16, stE)
            wuqe = sb('wuqe', [128, c.FCH, 128], BF16, stE)
            wdqe = sb('wdqe', [128, 2, 512], BF16, stE)
            srepe = sb('srepe', [128, 512], F32, stE)
            branch = sb('branchE', [128, 4, D], BF16, stE)
            xqte = junkV
            t1fe = junk2f
            for tc in range(NOWN // 512):
                for tt in range(4):
                    tile = tc * 4 + tt
                    tEs = b.esem(T)
                    V.wait(tEs, tEs.n)
                    ln_stats(res[:, tile, :], colB[:, 0:1], colB[:, 1:2],
                             cst[:, 1:2])
                    V.ts(t1fe[:, 0:D], res[:, tile, :], colB[:, 0:1], None, ALU.add)
                    V.red(colA[:, 6:7], t1fe[:, 0:D], ALU.max, abs_=True)
                    V.ts(colA[:, 7:8], colA[:, 6:7], colB[:, 1:2], op0=ALU.mult)
                    V.recip(colA[:, 8:9], colA[:, 7:8])
                    V.stt(colA[:, 9:10], colA[:, 8:9], 127.0, colB[:, 1:2],
                          ALU.mult, ALU.mult)
                    V.ts(invs[:, tile:tile + 1], colA[:, 7:8], 1.0 / 127.0)
                    V.ts(t1fe[:, 0:D], t1fe[:, 0:D], colA[:, 9:10], MAGIC,
                         ALU.mult, ALU.add)
                    V.ts(xqte[:, 0:D], t1fe[:, 0:D], MAGIC, None, ALU.subtract)
                    b.done(V)
                    T.wait(b.esem(V))
                    for f in range(c.FCH):
                        tr128(xqTe[:, f, tt * 128:(tt + 1) * 128],
                              xqte[:, f * 128:(f + 1) * 128], 6 + (f % 2), bf=True)
                for tt in range(4):
                    tile = tc * 4 + tt
                    b.pguard(6)
                    T.wait(b.esem(V))
                    T.tr(P[6][:, 0:128], repcol(invs[:, tile:tile + 1], 128), identf[:, :])
                    b.done(T)
                    V.wait(b.esem(T))
                    b.guard(('rowsc',), V)
                    V.copy(rowsc[0:1, 8 + tt * 128:8 + (tt + 1) * 128],
                           P[6][0:1, 0:128])
                    b.done(V)
                    b.pmark(6, V)
                b.pguard(6)
                T.wait(b.esem(V))
                T.mm(P[6][:, 0:512], onesf[0:1, 0:128], rowsc[0:1, 8:8 + 512])
                b.done(T)
                b.mark(('rowsc',), T)
                V.wait(b.esem(T))
                V.copy(srepe[:, :], P[6][:, 0:512])
                b.done(V)
                b.pmark(6, V)
                for o in range(c.NOC):
                    tEs = b.esem(T)
                    V.wait(tEs, tEs.n)
                    S.wait(tEs, tEs.n)
                    load_cast(wgqe, dram['WgT1'], 0, D, o * 128, 128, quant='sign')
                    load_cast(wuqe, dram['WuT1'], 0, D, o * 128, 128, quant='sign')
                    b.pguard(0)
                    b.pguard(1)
                    T.wait(b.esem(S))
                    T.wait(b.esem(V))
                    for f in range(c.FCH):
                        T.mm(P[0][:, 0:512], wgqe[:, f, :], xqTe[:, f, 0:512],
                             start=(f == 0), stop=(f == c.FCH - 1))
                    for f in range(c.FCH):
                        T.mm(P[1][:, 0:512], wuqe[:, f, :], xqTe[:, f, 0:512],
                             start=(f == 0), stop=(f == c.FCH - 1))
                    b.done(T)
                    V.wait(b.esem(T))
                    V.tt(junk2f[:, 0:512], P[0][:, 0:512], srepe[:, :], ALU.mult)
                    b.done(V)
                    b.pmark(0, V)
                    S.wait(b.esem(V))
                    S.act(junk2f[:, 0:512], junk2f[:, 0:512], AF.Silu)
                    b.done(S)
                    V.wait(b.esem(S))
                    V.tt(pTe[:, o, :], junk2f[:, 0:512], P[1][:, 0:512], ALU.mult)
                    b.done(V)
                    b.pmark(1, V)
                for op2 in range(max(1, D // 1024)):
                    for k in range(c.NOC):
                        for oc_i in range(min(2, D // 512)):
                            oc = op2 * 2 + oc_i
                            par = b.hist.get('_wq', 0) % 2
                            b.hist['_wq'] = b.hist.get('_wq', 0) + 1
                            b.guard(('wscr', par), Y)
                            Y.dma(wf32[:, par, 0:512],
                                  dram['WdT1'][k * 128:(k + 1) * 128,
                                               oc * 512:(oc + 1) * 512])
                            b.dinc(Y)
                            b.dwait(S)
                            tEs = b.esem(T)
                            S.wait(tEs, tEs.n)
                            S.act(wdqe[:, oc_i, :], wf32[:, par, 0:512], AF.Sign)
                            b.done(S)
                            b.mark(('wscr', par), S)
                        if k == 0:
                            for bk in range(8):
                                b.pguard(bk)
                        T.wait(b.esem(S))
                        T.wait(b.esem(V))
                        for oc_i in range(min(2, D // 512)):
                            for tb in range(4):
                                T.mm(P[oc_i * 4 + tb][:, 0:512],
                                     pTe[:, k, tb * 128:(tb + 1) * 128],
                                     wdqe[:, oc_i, :], start=(k == 0),
                                     stop=(k == c.NOC - 1))
                        b.done(T)
                    for oc_i in range(min(2, D // 512)):
                        oc = op2 * 2 + oc_i
                        for tb in range(4):
                            tile = tc * 4 + tb
                            bk = oc_i * 4 + tb
                            V.wait(b.esem(T))
                            V.ts(branch[:, tb, oc * 512:(oc + 1) * 512],
                                 P[bk][:, 0:512], invs[:, tile:tile + 1],
                                 None, ALU.mult, ALU.add,
                                 accum=bsums[:, tile * 4 + oc:tile * 4 + oc + 1])
                            b.done(V)
                            b.pmark(bk, V)
                            S.wait(b.esem(T))
                            S.act(junkS[:, 0:512], P[bk][:, 0:512],
                                  AF.Square, scale=invs[:, tile:tile + 1],
                                  accum=bsqs[:, tile * 4 + oc:tile * 4 + oc + 1])
                            b.done(S)
                            b.pmarkadd(bk, S)
                V.wait(b.esem(S))
                for tb in range(4):
                    branch_ln_add(branch, tb, tc * 4 + tb, nsum=D // 512)

        b.cut('pE')
        Y.wait(b.esem(V))
        Y.dma(y_out.rearrange('(t p) d -> p t d', p=128), res[:, :, :])
        dO.n += 16
        Y.last.then_inc(dO.h, 16)
        Y.wait(dO)

    return nc


def fwht_row_perm(dff):
    jhi = dff // 128
    g = 128 // jhi
    perm = np.empty(dff, np.int64)
    for d in range(dff // 128):
        for q in range(g):
            for ih in range(jhi):
                perm[d * 128 + q * jhi + ih] = ih * 128 + (d * g + q)
    return perm


def hadamard(n):
    h = np.array([[1.0]], np.float32)
    while h.shape[0] < n:
        h = np.block([[h, h], [h, -h]])
    return h.astype(np.float32)


def host_weights(cfg, inputs):
    import ml_dtypes
    c = cfg
    w = {}

    def f32(a):
        return np.ascontiguousarray(np.asarray(a), np.float32)
    w['w_qkv'] = f32(inputs['L0_attn_w_qkv'])
    w['w_out'] = f32(inputs['L0_attn_w_out'])
    w['ln_scale'] = f32(inputs['L0_attn_ln_scale']).reshape(1, -1)
    w['ln_bias'] = f32(inputs['L0_attn_ln_bias']).reshape(1, -1)
    w['wgT'] = f32(np.asarray(inputs['L0_ffn_w_gate']).T)
    w['wuT'] = f32(np.asarray(inputs['L0_ffn_w_up']).T)
    wdT = np.ascontiguousarray(np.asarray(inputs['L0_ffn_w_down']).T)
    w['wdTp'] = f32(wdT[fwht_row_perm(c.DFF)])
    w['norm_w'] = f32(inputs['L1_attn_norm_w']).reshape(1, -1)
    w['w_in'] = f32(inputs['L1_attn_w_in'])
    w['conv_w'] = f32(inputs['L1_attn_conv_w']).reshape(1, -1)
    w['conv_b'] = f32(inputs['L1_attn_conv_b']).reshape(1, -1)
    w['dt_bias'] = f32(inputs['L1_attn_dt_bias']).reshape(1, -1)
    w['A_log'] = f32(inputs['L1_attn_A_log']).reshape(1, -1)
    w['D_skip'] = f32(inputs['L1_attn_D_skip']).reshape(1, -1)
    w['gnorm_w'] = f32(inputs['L1_attn_gnorm_w']).reshape(1, -1)
    w['w_out1'] = f32(inputs['L1_attn_w_out'])
    w['wo_w'] = f32(inputs['L1_attn_wo_w'])
    w['WgT1'] = f32(np.asarray(inputs['L1_ffn_W_g']).T)
    w['WuT1'] = f32(np.asarray(inputs['L1_ffn_W_u']).T)
    w['WdT1'] = f32(np.asarray(inputs['L1_ffn_W_d']).T)
    w['H128'] = hadamard(128).astype(ml_dtypes.bfloat16)
    w['HJ'] = hadamard(c.NOC).astype(ml_dtypes.bfloat16)
    w['m_self'] = np.triu(np.ones((128, 128), np.float32)).astype(ml_dtypes.bfloat16)
    return w


def kernel(**inputs):
    cfg = CFG()
    x = np.asarray(inputs['L0_attn_x'], np.float32)
    B = x.shape[0]
    w = host_weights(cfg, inputs)
    nc = build_program(cfg)
    in_maps = []
    for core in range(cfg.n_cores):
        bb, h = core // 2, core % 2
        m = dict(w)
        own = x[bb, h * cfg.NOWN:(h + 1) * cfg.NOWN]
        oth = x[bb, (1 - h) * cfg.NOWN:(2 - h) * cfg.NOWN]
        m['x_seq'] = np.ascontiguousarray(np.concatenate([own, oth], 0))
        m['pos_off'] = np.full((128, 1), h * cfg.NOWN, np.float32)
        m['pos_off2'] = np.full((128, 1), (1 - h) * cfg.NOWN, np.float32)
        m['flag_h1'] = np.full((128, 1), float(h), np.float32)
        in_maps.append(m)
    r = run_bass_kernel_spmd(nc, in_maps, core_ids=list(range(cfg.n_cores)))
    out = np.zeros((B, cfg.SEQ, cfg.D), np.float32)
    for core in range(cfg.n_cores):
        bb, h = core // 2, core % 2
        out[bb, h * cfg.NOWN:(h + 1) * cfg.NOWN] = r.results[core]['y']
    return out
